# version 11
# baseline (speedup 1.0000x reference)
import sys
sys.path.insert(0, "/opt/trn_rl_repo")
from contextlib import ExitStack
import numpy as np
import concourse.bass as bass
import concourse.mybir as mybir
from concourse.bass_utils import run_bass_kernel_spmd

F32 = mybir.dt.float32
BF16 = mybir.dt.bfloat16
AF = mybir.ActivationFunctionType
ALU = mybir.AluOpType

D = 1024
NJ = 8
NS = 16
DTR = 64
T = 256
TG = 512
DH = 4096
MH = DH // 128
EPS = 1e-5

WOFF = {}
_off = 0
for _name, _n in (("w1", D * DH), ("w2", D * DH), ("h1", DH * DH),
                  ("h2", DH * DH), ("w3", DH * D)):
    WOFF[_name] = _off
    _off += _n
WTOT = _off

TILE = 128 * 1024
BLK = {"w1": 128 * 1024, "w2": 128 * 1024, "h1": 128 * 4096,
       "h2": 128 * 4096, "w3": 128 * 4096}
TILE_TAB = []
for _m in range(MH):
    TILE_TAB.append(("w1", _m, 0))
    TILE_TAB.append(("w2", _m, 0))
for _m in range(MH):
    for _q in range(4):
        TILE_TAB.append(("h1", _m, _q))
for _m in range(MH):
    for _q in range(4):
        TILE_TAB.append(("h2", _m, _q))
for _m in range(NJ):
    for _q in range(4):
        TILE_TAB.append(("w3", _m, _q))
NT = len(TILE_TAB)
assert NT * TILE == WTOT

QKEYS = ("sq", "aq", "gq")
ALLKEYS = ("pe", "ve", "se", "po", "sq", "aq", "gq")


class Res:
    __slots__ = ("w", "r")
    def __init__(self):
        self.w = None
        self.r = []


class Eng:
    SELF_SYNC = ("ve", "se", "po")

    def __init__(self, key, cnt, default_q=None):
        self.key = key
        self.cnt = cnt
        self.ops = []
        self.hi = {}
        self.default_q = default_q
        self.self_sync = key in Eng.SELF_SYNC
        self.pending = []

    def _wait(self, ev):
        k, v = ev
        if k == self.key and not self.self_sync:
            return
        if self.hi.get(k, 0) >= v:
            return
        self.hi[k] = v
        self.ops.append(("w", k, v))

    def acquire(self, rd=(), wr=()):
        for b in rd:
            if b.w is not None:
                self._wait(b.w)
        for b in wr:
            if b.w is not None:
                self._wait(b.w)
            for ev in b.r:
                self._wait(ev)

    def op(self, fn, rd=(), wr=(), inc=True, q=None):
        self.acquire(rd=rd, wr=wr)
        q = q if q is not None else self.default_q
        key = q if q is not None else self.key
        step = 16 if q is not None else 1
        self.ops.append(("o", fn, key if inc else None))
        if not inc:
            assert not wr, "non-inc ops must not declare writes"
            self.pending.extend(rd)
            return None
        self.cnt[key] += step
        ev = (key, self.cnt[key])
        for b in rd:
            b.r.append(ev)
        for b in self.pending:
            b.r.append(ev)
        self.pending = []
        for b in wr:
            b.w = ev
            b.r = []
        return ev

    def wait_cnt(self, key):
        self._wait((key, self.cnt[key]))


def build_program(seq_len):
    NCH = seq_len // T
    NG = NCH // 2
    assert NCH % 2 == 0
    nc = bass.Bass("TRN2", target_bir_lowering=False,
                   detect_race_conditions=False)

    x_h = nc.dram_tensor("x", [seq_len, D], F32, kind="ExternalInput").ap()
    y_h = nc.dram_tensor("y", [seq_len, D], F32, kind="ExternalOutput").ap()
    wflat = nc.dram_tensor("wflat", [WTOT], F32, kind="ExternalInput").ap()
    xpw_h = nc.dram_tensor("xpw", [128, NJ * 128], F32, kind="ExternalInput").ap()
    dtw_h = nc.dram_tensor("dtw", [DTR, NJ * 128], F32, kind="ExternalInput").ap()
    alog_h = nc.dram_tensor("alog", [128, NS], F32, kind="ExternalInput").ap()
    dtb_h = nc.dram_tensor("dtb", [128, NJ], F32, kind="ExternalInput").ap()
    dvec_h = nc.dram_tensor("dvec", [128, NJ], F32, kind="ExternalInput").ap()
    s1_h = nc.dram_tensor("s1v", [128, MH], F32, kind="ExternalInput").ap()
    s2_h = nc.dram_tensor("s2v", [128, MH], F32, kind="ExternalInput").ap()
    q1_h = nc.dram_tensor("q1v", [128, MH], F32, kind="ExternalInput").ap()
    q2_h = nc.dram_tensor("q2v", [128, MH], F32, kind="ExternalInput").ap()
    s3_h = nc.dram_tensor("s3v", [128, NJ], F32, kind="ExternalInput").ap()
    eye_h = nc.dram_tensor("eye", [128, 128], F32, kind="ExternalInput").ap()
    ones_h = nc.dram_tensor("onescol", [128, 1], F32, kind="ExternalInput").ap()
    wbf = nc.dram_tensor("wbf", [WTOT], BF16, kind="Internal").ap()
    rowd = nc.dram_tensor("rowd", [4, T], F32, kind="Internal").ap()
    bcd = nc.dram_tensor("bcd", [4, NS * T], BF16, kind="Internal").ap()

    ctx = ExitStack()
    def sbuf(name, shape, dt=F32):
        return ctx.enter_context(nc.sbuf_tensor(name, shape, dt))
    def psum(name, shape, dt=F32):
        return ctx.enter_context(nc.psum_tensor(name, shape, dt))

    with ctx:
        x_tm = sbuf("x_tm", [128, D])
        u_tm = sbuf("u_tm", [128, D])
        x2 = [sbuf(f"x2_{i}", [128, NJ * T]) for i in range(4)]
        u_ln = [sbuf(f"uln{i}", [128, NJ * T]) for i in range(2)]
        ulnb = [sbuf(f"ulnb{i}", [128, T], BF16) for i in range(2)]
        delta = [sbuf(f"delta{i}", [128, NJ * T]) for i in range(2)]
        wtile = [sbuf(f"wt{i}", [128, T]) for i in range(2)]
        dBu = [sbuf(f"dBu{i}", [128, T]) for i in range(2)]
        htile = [sbuf(f"ht{i}", [128, T]) for i in range(2)]
        b_all = sbuf("ball", [128, NS * T], BF16)
        c_all = sbuf("call", [128, NS * T], BF16)
        misc = [sbuf(f"misc{i}", [128, T]) for i in range(4)]
        zbuf = [sbuf(f"zb{i}", [128, T]) for i in range(2)]
        state = sbuf("state", [128, 128])
        sb6 = sbuf("sb6", [128, 20])
        mv = sbuf("mv", [128, 2])
        rstd1 = sbuf("rstd1", [128, 1])
        mr1 = sbuf("mr1", [128, 1])
        bcmp = [sbuf(f"bcmp{i}", [NS, T], BF16) for i in range(2)]
        ccmp = [sbuf(f"ccmp{i}", [NS, T], BF16) for i in range(2)]
        dtraw = [sbuf(f"dtraw{i}", [DTR, T], BF16) for i in range(2)]
        rows = sbuf("rows", [1, 2 * T])
        rrep = [sbuf(f"rrep{i}", [128, T]) for i in range(2)]
        ln2b = sbuf("ln2b", [128, NJ * TG], BF16)
        hbuf = sbuf("hbuf", [128, MH * TG], BF16)
        g1buf = sbuf("g1buf", [128, MH * TG], BF16)
        sa = [sbuf(f"sa{i}", [128, TG]) for i in range(1)]
        wslab = sbuf("wslab", [128, 8 * 1024], BF16)
        outev = [sbuf(f"outev{i}", [128, 128]) for i in range(2)]
        xpw_b = sbuf("xpwb", [128, NJ * 128], BF16)
        dtw_b = sbuf("dtwb", [DTR, NJ * 128], BF16)
        arep = sbuf("arep", [128, NS])
        dtb_s = sbuf("dtbs", [128, NJ])
        dvec_s = sbuf("dvecs", [128, NJ])
        s1_s = sbuf("s1s", [128, MH])
        s2_s = sbuf("s2s", [128, MH])
        q1_s = sbuf("q1s", [128, MH])
        q2_s = sbuf("q2s", [128, MH])
        s3_s = sbuf("s3s", [128, NJ])
        eye_s = sbuf("eyes", [128, 128])
        ones_s = sbuf("oness", [128, 1])
        eps_v = sbuf("epsv", [128, 1])

        wsl = [wslab[:, i * 1024:(i + 1) * 1024] for i in range(8)]
        wslab_f = wslab[:].bitcast(F32)
        wsl_f32 = [wslab_f[:, p * 1024:(p + 1) * 1024] for p in range(2)]

        pf = [psum(f"pf{i}", [128, TG]) for i in range(4)]
        pebt = psum("pebt", [128, TG])
        pdt = psum("pdt", [128, TG])
        pp = [psum(f"pp{i}", [128, TG]) for i in range(2)]
        pT = [pebt[:, 0:128], pebt[:, 128:256]]
        pX = pebt[:, 256:512]
        pD = [pdt[:, 0:256], pdt[:, 256:512]]

        sems = {k: ctx.enter_context(nc.semaphore(k)) for k in ALLKEYS}
        CNT = {k: 0 for k in ALLKEYS}
        PE = Eng("pe", CNT)
        VE = Eng("ve", CNT)
        SE = Eng("se", CNT)
        PO = Eng("po", CNT)
        SQ = Eng("sq", CNT, default_q="sq")

        r_xtm = Res()
        r_utm = Res()
        r_zb = [Res() for _ in range(2)]
        r_x2 = [Res() for _ in range(4)]
        r_uln = [Res() for _ in range(2)]
        r_ulnb = [Res() for _ in range(2)]
        r_delta = [Res() for _ in range(2)]
        r_wtile = [Res() for _ in range(2)]
        r_dBu = [Res() for _ in range(2)]
        r_h = [Res() for _ in range(2)]
        r_ball = Res()
        r_call = Res()
        r_misc = [Res() for _ in range(4)]
        r_state = Res()
        r_stat = Res()
        r_bcmp = [Res() for _ in range(2)]
        r_ccmp = [Res() for _ in range(2)]
        r_dtraw = [Res() for _ in range(2)]
        r_rows = Res()
        r_rowd = Res()
        r_bcd = [Res() for _ in range(2)]
        r_rrep = [Res() for _ in range(2)]
        r_ln2b = Res()
        r_hbuf = [Res() for _ in range(MH)]
        r_g1 = [Res() for _ in range(MH)]
        r_sa = [Res() for _ in range(1)]
        r_wsl = [Res() for _ in range(8)]
        r_fp = [Res() for _ in range(2)]
        r_outev = [Res() for _ in range(2)]
        r_pf = [Res() for _ in range(4)]
        r_pT = [Res() for _ in range(2)]
        r_pX = Res()
        r_pD = [Res() for _ in range(2)]
        r_pp = [Res() for _ in range(2)]
        r_consts = Res()

        for dst, src in ((arep[:], alog_h), (dtb_s[:], dtb_h), (dvec_s[:], dvec_h),
                         (s1_s[:], s1_h), (s2_s[:], s2_h), (q1_s[:], q1_h),
                         (q2_s[:], q2_h), (s3_s[:], s3_h),
                         (eye_s[:], eye_h), (ones_s[:], ones_h)):
            SQ.op(lambda e, d=dst, s=src: e.dma_start(out=d, in_=s), wr=[r_consts])
        SQ.op(lambda e: e.dma_start(out=x2[2][:, 0:1024], in_=xpw_h), wr=[r_x2[2]])
        SE.op(lambda e: e.copy(xpw_b[:], x2[2][:, 0:1024]),
              rd=[r_x2[2]], wr=[r_consts])
        SQ.op(lambda e: e.dma_start(out=x2[3][0:DTR, 0:1024], in_=dtw_h),
              wr=[r_x2[3]])
        SE.op(lambda e: e.copy(dtw_b[:], x2[3][0:DTR, 0:1024]),
              rd=[r_x2[3]], wr=[r_consts])
        SE.op(lambda e: e.activation(arep[:], arep[:], AF.Exp),
              rd=[r_consts], wr=[r_consts])
        VE.op(lambda e: e.tensor_scalar_mul(arep[:], arep[:], -1.0),
              rd=[r_consts], wr=[r_consts])
        VE.op(lambda e: e.memset(state[:], 0.0), wr=[r_state])
        VE.op(lambda e: e.memset(eps_v[:], EPS), wr=[r_consts])

        def tile_src_f32(i):
            name, m, q = TILE_TAB[i]
            base = WOFF[name] + m * BLK[name]
            ap = wflat[base: base + BLK[name]].rearrange("(p f) -> p f", p=128)
            return ap[:, q * 1024:(q + 1) * 1024]

        def tile_wbf(i):
            return wbf[i * TILE:(i + 1) * TILE].rearrange("(p f) -> p f", p=128)

        store_ev = [None] * NT

        class WStream:
            def __init__(self, g):
                self.g = g
                self.next_p = 0
                self.next_c = 0

            def produce_one(self):
                i = self.next_p
                self.next_p += 1
                if i >= NT:
                    return
                if self.g == 0:
                    p = i % 2
                    oslot = 4 + i % 4
                    src = tile_src_f32(i)
                    SQ.op(lambda e, src=src, p=p: e.dma_start(
                        out=wsl_f32[p], in_=src), wr=[r_fp[p]])
                    PO.op(lambda e, p=p, oslot=oslot: e.tensor_copy(
                        out=wsl[oslot], in_=wsl_f32[p]),
                        rd=[r_fp[p]], wr=[r_wsl[oslot]])
                    store_ev[i] = SE.op(lambda e, i=i, oslot=oslot: e.dma_start(
                        out=tile_wbf(i), in_=wsl[oslot]),
                        rd=[r_wsl[oslot]], q="aq")
                else:
                    slot = i % 8
                    if self.g == 1:
                        SQ._wait(store_ev[i])
                    SQ.op(lambda e, i=i, slot=slot: e.dma_start(
                        out=wsl[slot], in_=tile_wbf(i)), wr=[r_wsl[slot]])

            def consume(self):
                i = self.next_c
                self.next_c += 1
                la = 4 if self.g == 0 else 7
                while self.next_p < min(i + la, NT):
                    self.produce_one()
                slot = (4 + i % 4) if self.g == 0 else i % 8
                return wsl[slot], r_wsl[slot]

        def merge_g0_res():
            for p in range(2):
                for sl in (2 * p, 2 * p + 1):
                    r_wsl[sl].w = r_fp[p].w
                    r_wsl[sl].r = list(r_fp[p].r)

        def prep_chunk(k):
            p = k % 2
            x2s, rx2 = x2[k % 4], r_x2[k % 4]
            for c in range(2):
                src_ = x_h[k * T + c * 128: k * T + (c + 1) * 128, :]
                PO.op(lambda e, s=src_: e.dma_start(out=x_tm[:], in_=s),
                      rd=[], wr=[r_xtm], q="gq")
                xt = x_tm
                VE.op(lambda e, xt=xt: e.bn_stats(sb6[:, 0:6], xt[:, 0:512]),
                      rd=[r_xtm], wr=[r_stat])
                VE.op(lambda e, xt=xt: e.bn_stats(sb6[:, 6:12], xt[:, 512:1024]),
                      rd=[r_xtm], wr=[r_stat])
                s6v = sb6[:, 0:12].rearrange("p (g k) -> p g k", k=3)
                means_ap = s6v[:, :, 1]
                cv_ap = s6v[:, :, 2]
                VE.op(lambda e, means_ap=means_ap: e.reduce_sum(
                    mv[:, 0:1], means_ap, mybir.AxisListType.X),
                    rd=[r_stat], wr=[r_stat])
                VE.op(lambda e, means_ap=means_ap: e.tensor_mul(
                    sb6[:, 12:16], means_ap, means_ap), rd=[r_stat], wr=[r_stat])
                VE.op(lambda e: e.reduce_sum(
                    sb6[:, 17:18], sb6[:, 12:16], mybir.AxisListType.X),
                    rd=[r_stat], wr=[r_stat])
                VE.op(lambda e, cv_ap=cv_ap: e.reduce_sum(
                    sb6[:, 16:17], cv_ap, mybir.AxisListType.X),
                    rd=[r_stat], wr=[r_stat])
                VE.op(lambda e: e.tensor_scalar_mul(mv[:, 0:1], mv[:, 0:1], 0.25),
                      rd=[r_stat], wr=[r_stat])
                VE.op(lambda e: e.tensor_scalar_mul(sb6[:, 18:19], sb6[:, 16:17],
                                                    1.0 / D),
                      rd=[r_stat], wr=[r_stat])
                VE.op(lambda e: e.scalar_tensor_tensor(
                    mv[:, 1:2], sb6[:, 17:18], 0.25, sb6[:, 18:19],
                    ALU.mult, ALU.add), rd=[r_stat], wr=[r_stat])
                VE.op(lambda e: e.tensor_mul(sb6[:, 19:20], mv[:, 0:1], mv[:, 0:1]),
                      rd=[r_stat], wr=[r_stat])
                VE.op(lambda e: e.tensor_sub(mv[:, 1:2], mv[:, 1:2], sb6[:, 19:20]),
                      rd=[r_stat], wr=[r_stat])
                for j in range(NJ):
                    bank = j % 2
                    PE.acquire(rd=[r_xtm, r_consts], wr=[r_pp[bank]])
                    PE.op(lambda e, j=j, bank=bank: e.transpose(
                        pp[bank][:, 0:128], x_tm[:, j * 128:(j + 1) * 128],
                        eye_s[:]),
                        rd=[r_xtm], wr=[r_pp[bank]])
                    SE.op(lambda e, j=j, c=c, bank=bank, x2s=x2s: e.copy(
                        x2s[:, j * T + c * 128: j * T + (c + 1) * 128],
                        pp[bank][:, 0:128]),
                        rd=[r_pp[bank]], wr=[rx2])
                SE.op(lambda e: e.activation(rstd1[:], mv[:, 1:2], AF.Ln, bias=eps_v[:, 0:1]),
                      rd=[r_stat], wr=[r_stat])
                SE.op(lambda e: e.activation(rstd1[:], rstd1[:], AF.Exp, scale=-0.5),
                      rd=[r_stat], wr=[r_stat])
                VE.op(lambda e: e.scalar_tensor_tensor(mr1[:], mv[:, 0:1], -1.0,
                                                       rstd1[:], ALU.mult, ALU.mult),
                      rd=[r_stat], wr=[r_stat])
                VE.op(lambda e, xt=xt: e.tensor_scalar(
                    u_tm[:], xt[:], rstd1[:, 0:1], mr1[:, 0:1], ALU.mult, ALU.add),
                    rd=[r_stat, r_xtm], wr=[r_utm])
                for j in range(NJ):
                    bank = j % 2
                    PE.acquire(rd=[r_utm, r_consts], wr=[r_pp[bank]])
                    PE.op(lambda e, j=j, bank=bank: e.transpose(
                        pp[bank][:, 128:256], u_tm[:, j * 128:(j + 1) * 128],
                        eye_s[:]),
                        rd=[r_utm], wr=[r_pp[bank]])
                    SE.op(lambda e, j=j, c=c, p=p, bank=bank: e.copy(
                        u_ln[p][:, j * T + c * 128: j * T + (c + 1) * 128],
                        pp[bank][:, 128:256]),
                        rd=[r_pp[bank]], wr=[r_uln[p]])
            for j in range(NJ):
                SE.op(lambda e, j=j, p=p: e.copy(
                    ulnb[j % 2][:], u_ln[p][:, j * T:(j + 1) * T]),
                    rd=[r_uln[p]], wr=[r_ulnb[j % 2]])
                if j == 0:
                    PE.acquire(wr=[r_pX])
                PE.op(lambda e, j=j: e.matmul(
                    pX, xpw_b[:, j * 128:(j + 1) * 128],
                    ulnb[j % 2][:], start=(j == 0), stop=(j == NJ - 1)),
                    rd=[r_ulnb[j % 2], r_consts],
                    wr=[r_pX] if j == NJ - 1 else (), inc=True)
            SE.op(lambda e, p=p: e.copy(bcmp[p][:], pebt[0:NS, 256:512]),
                  rd=[r_pX], wr=[r_bcmp[p]])
            SE.op(lambda e, p=p: e.copy(ccmp[p][:], pebt[32:48, 256:512]),
                  rd=[r_pX], wr=[r_ccmp[p]])
            SE.op(lambda e, p=p: e.copy(dtraw[p][:], pebt[64:128, 256:512]),
                  rd=[r_pX], wr=[r_dtraw[p]])
            SE.op(lambda e, p=p: e.dma_start(
                out=bcd[2 * p:2 * p + 1, :].rearrange("o (n t) -> (o n) t", t=T),
                in_=bcmp[p][:]),
                rd=[r_bcmp[p]], wr=[r_bcd[p]], q="aq")
            SE.op(lambda e, p=p: e.dma_start(
                out=bcd[2 * p + 1:2 * p + 2, :].rearrange("o (n t) -> (o n) t", t=T),
                in_=ccmp[p][:]),
                rd=[r_ccmp[p]], wr=[r_bcd[p]], q="aq")
            for j in range(NJ):
                dslot = j % 2
                ma, rma = misc[2 * (j % 2)], r_misc[2 * (j % 2)]
                mb, rmb = misc[2 * (j % 2) + 1], r_misc[2 * (j % 2) + 1]
                PE.op(lambda e, j=j, p=p, dslot=dslot: e.matmul(
                    pD[dslot], dtw_b[:, j * 128:(j + 1) * 128], dtraw[p][:],
                    start=True, stop=True),
                    rd=[r_dtraw[p], r_consts], wr=[r_pD[dslot]])
                VE.op(lambda e, j=j, dslot=dslot, ma=ma: e.tensor_scalar(
                    ma[:], pD[dslot], dtb_s[:, j:j + 1], None,
                    ALU.add),
                    rd=[r_pD[dslot], r_consts], wr=[rma])
                SE.op(lambda e, ma=ma, mb=mb: e.activation(mb[:], ma[:], AF.Relu),
                      rd=[rma], wr=[rmb])
                SE.op(lambda e, ma=ma: e.activation(ma[:], ma[:], AF.Abs),
                      rd=[rma], wr=[rma])
                SE.op(lambda e, ma=ma: e.activation(ma[:], ma[:], AF.Exp, scale=-1.0),
                      rd=[rma], wr=[rma])
                SE.op(lambda e, ma=ma: e.activation(ma[:], ma[:], AF.Ln, bias=1.0),
                      rd=[rma], wr=[rma])
                VE.op(lambda e, j=j, p=p, ma=ma, mb=mb: e.tensor_add(
                    delta[p][:, j * T:(j + 1) * T], mb[:], ma[:]),
                    rd=[rmb, rma], wr=[r_delta[p]])

        def visit_steps(k, j):
            p = k % 2
            js = j % 2
            x2s, rx2 = x2[k % 4], r_x2[k % 4]
            jT = slice(j * T, (j + 1) * T)
            ADD_ENG = VE if k < 2 else PO

            def head():
                VE.op(lambda e: e.tensor_mul(
                    wtile[js][:], delta[p][:, jT], u_ln[p][:, jT]),
                    rd=[r_delta[p], r_uln[p]], wr=[r_wtile[js]])
                VE.op(lambda e: e.scalar_tensor_tensor(
                    x2s[:, jT], u_ln[p][:, jT], dvec_s[:, j:j + 1], x2s[:, jT],
                    ALU.mult, ALU.add),
                    rd=[r_uln[p], r_consts, rx2], wr=[rx2])

            def emit_exp(n):
                bank = (n // 2) % 2
                dcol = slice((n % 2) * T, (n % 2 + 1) * T)
                SE.op(lambda e, dcol=dcol, n=n, bank=bank: e.activation(
                    pp[bank][:, dcol], delta[p][:, jT], AF.Exp,
                    scale=arep[:, n:n + 1]),
                    rd=[r_delta[p], r_consts], wr=[r_pp[bank]])

            def emit_scan(n):
                bank = (n // 2) % 2
                dcol = slice((n % 2) * T, (n % 2 + 1) * T)
                col = j * NS + n
                db, rdb = dBu[n % 2], r_dBu[n % 2]
                hs, rh = htile[n % 2], r_h[n % 2]
                PO.op(lambda e, nT=slice(n * T, (n + 1) * T), db=db:
                      e.tensor_mul(db[:], wtile[js][:], b_all[:, nT]),
                      rd=[r_wtile[js], r_ball], wr=[rdb])
                VE.op(lambda e, dcol=dcol, bank=bank, db=db, hs=hs,
                      col=col: e.tensor_tensor_scan(
                    hs[:], pp[bank][:, dcol], db[:], state[:, col:col + 1],
                    ALU.mult, ALU.add),
                    rd=[r_pp[bank], rdb, r_state], wr=[rh])

            def emit_z(n):
                zs = n % 2
                hs, rh = htile[n % 2], r_h[n % 2]
                PO.op(lambda e, nT=slice(n * T, (n + 1) * T), hs=hs, zs=zs:
                      e.tensor_mul(zbuf[zs][:], hs[:], c_all[:, nT]),
                      rd=[rh, r_call], wr=[r_zb[zs]])
                PO.op(lambda e, hs=hs, col=j * NS + n: e.tensor_copy(
                    out=state[:, col:col + 1], in_=hs[:, T - 1:T]),
                    rd=[rh], wr=[r_state])

            def emit_add(n):
                zs = n % 2
                ADD_ENG.op(lambda e, zs=zs: e.tensor_add(
                    x2s[:, jT], x2s[:, jT], zbuf[zs][:]),
                    rd=[r_zb[zs], rx2], wr=[rx2])

            steps = [head, lambda: (emit_exp(0), emit_exp(1)),
                     lambda: (emit_exp(2), emit_exp(3))]
            for n in range(NS):
                def mid(n=n):
                    emit_scan(n)
                    if n >= 1:
                        emit_z(n - 1)
                    if n >= 2:
                        emit_add(n - 2)
                        if n + 2 < NS:
                            emit_exp(n + 2)
                steps.append(mid)
            def tail():
                emit_z(NS - 1)
                emit_add(NS - 2)
                emit_add(NS - 1)
            steps.append(tail)
            return steps

        def chunk_steps(k):
            p = k % 2
            steps = []
            def bc_bcast():
                PO.op(lambda e: e.dma_start(
                    out=b_all[:],
                    in_=bcd[2 * p:2 * p + 1, :].broadcast_to([128, NS * T])),
                    rd=[r_bcd[p]], wr=[r_ball], q="gq")
                PO.op(lambda e: e.dma_start(
                    out=c_all[:],
                    in_=bcd[2 * p + 1:2 * p + 2, :].broadcast_to([128, NS * T])),
                    rd=[r_bcd[p]], wr=[r_call], q="gq")
            steps.append(bc_bcast)
            for j in range(NJ):
                steps.extend(visit_steps(k, j))
            return steps

        def pace(it, n):
            done = 0
            for fn in it:
                fn()
                done += 1
                if done >= n:
                    break

        def ln2_chunk(k):
            p = k % 2
            x2s, rx2 = x2[k % 4], r_x2[k % 4]

            def s_sum():
                PE.acquire(wr=[r_pD[0], r_pD[1]])
                for j in range(NJ):
                    jT = slice(j * T, (j + 1) * T)
                    PE.op(lambda e, jT=jT, j=j: e.matmul(
                        pdt[0:1, 0:T], ones_s[:, 0:1], x2s[:, jT],
                        start=(j == 0), stop=(j == NJ - 1)),
                        rd=[rx2, r_consts], wr=(), inc=(j == NJ - 1))

            def s_sq():
                for j in range(NJ):
                    jT = slice(j * T, (j + 1) * T)
                    sq, rsq = misc[j % 4], r_misc[j % 4]
                    SE.op(lambda e, jT=jT, sq=sq: e.activation(
                        sq[:], x2s[:, jT], AF.Square), rd=[rx2], wr=[rsq])
                    PE.op(lambda e, j=j, sq=sq: e.matmul(
                        pdt[0:1, 256:256 + T], ones_s[:, 0:1], sq[:],
                        start=(j == 0), stop=(j == NJ - 1)),
                        rd=[rsq, r_consts],
                        wr=[r_pD[0], r_pD[1]] if j == NJ - 1 else (),
                        inc=(j == NJ - 1))

            def s_rows():
                r0 = rows[0:1, 0:T]
                r1_ = rows[0:1, T:2 * T]
                r2 = rrep[0][0:1, 0:T]
                SE.op(lambda e: e.copy(rows[0:1, 0:T], pdt[0:1, 0:T]),
                      rd=[r_pD[0]], wr=[r_rows])
                SE.op(lambda e: e.copy(rows[0:1, T:2 * T], pdt[0:1, 256:256 + T]),
                      rd=[r_pD[1]], wr=[r_rows])
                VE.op(lambda e: e.tensor_scalar_mul(r0, r0, 1.0 / D),
                      rd=[r_rows], wr=[r_rows])
                VE.op(lambda e: e.tensor_mul(r2, r0, r0),
                      rd=[r_rows], wr=[r_rrep[0]])
                VE.op(lambda e: e.scalar_tensor_tensor(r1_, r1_, 1.0 / D, r2,
                                                       ALU.mult, ALU.subtract),
                      rd=[r_rows, r_rrep[0]], wr=[r_rows])
                SE.op(lambda e: e.activation(r1_, r1_, AF.Ln, bias=eps_v[0:1, 0:1]),
                      rd=[r_rows], wr=[r_rows])
                SE.op(lambda e: e.activation(r1_, r1_, AF.Exp, scale=-0.5),
                      rd=[r_rows], wr=[r_rows])
                VE.op(lambda e: e.scalar_tensor_tensor(r2, r0, -1.0, r1_,
                                                       ALU.mult, ALU.mult),
                      rd=[r_rows], wr=[r_rrep[0]])
                SE.op(lambda e: e.dma_start(out=rowd[2 * p:2 * p + 1, :], in_=r1_),
                      rd=[r_rows], wr=[r_rowd], q="aq")
                SE.op(lambda e: e.dma_start(out=rowd[2 * p + 1:2 * p + 2, :], in_=r2),
                      rd=[r_rows, r_rrep[0]], wr=[r_rowd], q="aq")
                SE.op(lambda e: e.dma_start(
                    out=rrep[0][:],
                    in_=rowd[2 * p:2 * p + 1, :].broadcast_to([128, T])),
                    rd=[r_rowd], wr=[r_rrep[0]], q="aq")
                SE.op(lambda e: e.dma_start(
                    out=rrep[1][:],
                    in_=rowd[2 * p + 1:2 * p + 2, :].broadcast_to([128, T])),
                    rd=[r_rowd], wr=[r_rrep[1]], q="aq")

            def s_scale(j0, j1):
                for j in range(j0, j1):
                    jT = slice(j * T, (j + 1) * T)
                    tmp, rtmp = misc[j % 4], r_misc[j % 4]
                    VE.op(lambda e, jT=jT, tmp=tmp: e.tensor_mul(
                        tmp[:], x2s[:, jT], rrep[0][:]),
                        rd=[rx2, r_rrep[0]], wr=[rtmp])
                    dst = ln2b[:, j * TG + p * T: j * TG + (p + 1) * T]
                    VE.op(lambda e, tmp=tmp, dst=dst: e.tensor_add(
                        dst, tmp[:], rrep[1][:]),
                        rd=[rtmp, r_rrep[1]], wr=[r_ln2b])

            return [s_sum, s_sq, s_rows,
                    lambda: s_scale(0, 4), lambda: s_scale(4, NJ)]

        def ffn_w1w2(g, st):
            for m in range(MH):
                wv1, rw1 = st.consume()
                pb1, pb2 = m % 2, 2 + m % 2
                PE.acquire(wr=[r_pf[pb1]])
                for k in range(NJ):
                    last = k == NJ - 1
                    PE.op(lambda e, k=k, pb1=pb1, wv1=wv1, last=last: e.matmul(
                        pf[pb1][:], wv1[:, k * 128:(k + 1) * 128],
                        ln2b[:, k * TG:(k + 1) * TG], start=(k == 0), stop=last),
                        rd=[rw1, r_ln2b], wr=[r_pf[pb1]] if last else (),
                        inc=last)
                wv2, rw2 = st.consume()
                PE.acquire(wr=[r_pf[pb2]])
                for k in range(NJ):
                    last = k == NJ - 1
                    PE.op(lambda e, k=k, pb2=pb2, wv2=wv2, last=last: e.matmul(
                        pf[pb2][:], wv2[:, k * 128:(k + 1) * 128],
                        ln2b[:, k * TG:(k + 1) * TG], start=(k == 0), stop=last),
                        rd=[rw2, r_ln2b], wr=[r_pf[pb2]] if last else (),
                        inc=last)
                SE.op(lambda e, m=m, pb1=pb1: e.activation(
                    sa[0][:], pf[pb1][:], AF.Silu, scale=s1_s[:, m:m + 1]),
                    rd=[r_pf[pb1], r_consts], wr=[r_sa[0]])
                VE.op(lambda e, m=m, pb2=pb2: e.scalar_tensor_tensor(
                    hbuf[:, m * TG:(m + 1) * TG], pf[pb2][:], s2_s[:, m:m + 1],
                    sa[0][:], ALU.mult, ALU.mult),
                    rd=[r_pf[pb2], r_consts, r_sa[0]], wr=[r_hbuf[m]])

        def ffn_h(g, st, src_res, src_buf, dst_res, dst_buf, qscale, steps_it):
            for m in range(MH):
                pb = m % 4
                PE.acquire(wr=[r_pf[pb]])
                for quarter in range(4):
                    wv, rw = st.consume()
                    for kk in range(MH // 4):
                        k = quarter * (MH // 4) + kk
                        last = k == MH - 1
                        PE.op(lambda e, kk=kk, k=k, pb=pb, wv=wv, last=last,
                              sb_=src_buf: e.matmul(
                            pf[pb][:], wv[:, kk * 128:(kk + 1) * 128],
                            sb_[:, k * TG:(k + 1) * TG], start=(k == 0), stop=last),
                            rd=[rw, src_res[k]],
                            wr=[r_pf[pb]] if last else (), inc=last)
                SE.op(lambda e, m=m, pb=pb, db=dst_buf, qs=qscale: e.activation(
                    db[:, m * TG:(m + 1) * TG], pf[pb][:], AF.Gelu,
                    scale=qs[:, m:m + 1]),
                    rd=[r_pf[pb], r_consts], wr=[dst_res[m]])
                pace(steps_it, 6)
            pace(steps_it, 10 ** 9)

        def ffn_w3(g, st, steps_it):
            def epilogue(m, pb):
                for c in range(2):
                    kc = 2 * g + c
                    x2s, rx2 = x2[kc % 4], r_x2[kc % 4]
                    VE.op(lambda e, m=m, c=c, pb=pb, x2s=x2s: e.scalar_tensor_tensor(
                        x2s[:, m * T:(m + 1) * T], pf[pb][:, c * T:(c + 1) * T],
                        s3_s[:, m:m + 1], x2s[:, m * T:(m + 1) * T],
                        ALU.mult, ALU.add),
                        rd=[r_pf[pb], r_consts, rx2], wr=[rx2])
                for cb in range(4):
                    c, ci = cb // 2, cb % 2
                    kc = 2 * g + c
                    x2s, rx2 = x2[kc % 4], r_x2[kc % 4]
                    bank = cb % 2
                    PE.acquire(rd=[rx2, r_consts], wr=[r_pp[bank]])
                    PE.op(lambda e, m=m, ci=ci, bank=bank, x2s=x2s: e.transpose(
                        pp[bank][:, 0:128],
                        x2s[:, m * T + ci * 128: m * T + (ci + 1) * 128], eye_s[:]),
                        rd=[rx2, r_consts], wr=[r_pp[bank]])
                    SE.op(lambda e, cb=cb, bank=bank: e.copy(
                        outev[cb % 2][:], pp[bank][:, 0:128]),
                        rd=[r_pp[bank]], wr=[r_outev[cb % 2]])
                    dst = y_h[g * TG + cb * 128: g * TG + (cb + 1) * 128,
                              m * 128:(m + 1) * 128]
                    SE.op(lambda e, d=dst, cb=cb: e.dma_start(
                        out=d, in_=outev[cb % 2][:]),
                        rd=[r_outev[cb % 2]], q="aq")

            pend = None
            for m in range(NJ):
                pb = m % 2
                PE.acquire(wr=[r_pf[pb]])
                for quarter in range(4):
                    wv, rw = st.consume()
                    for kk in range(MH // 4):
                        k = quarter * (MH // 4) + kk
                        last = k == MH - 1
                        PE.op(lambda e, kk=kk, k=k, pb=pb, wv=wv,
                              last=last: e.matmul(
                            pf[pb][:], wv[:, kk * 128:(kk + 1) * 128],
                            hbuf[:, k * TG:(k + 1) * TG], start=(k == 0), stop=last),
                            rd=[rw, r_hbuf[k]],
                            wr=[r_pf[pb]] if last else (), inc=last)
                if pend is not None:
                    pend()
                pend = (lambda m=m, pb=pb: epilogue(m, pb))
                pace(steps_it, 2)
            pend()
            pace(steps_it, 10 ** 9)

        prep_chunk(0)
        for fn in chunk_steps(0):
            fn()
        for fn in ln2_chunk(0):
            fn()
        prep_chunk(1)
        for fn in chunk_steps(1):
            fn()
        for fn in ln2_chunk(1):
            fn()

        for g in range(NG):
            st = WStream(g)
            a, b = 2 * g + 2, 2 * g + 3
            ffn_w1w2(g, st)
            if a < NCH:
                prep_chunk(a)
                prep_chunk(b)
                it_a = iter(chunk_steps(a))
                it_b = iter(ln2_chunk(a) + chunk_steps(b))
                it_c = iter(ln2_chunk(b))
            else:
                it_a = it_b = it_c = iter(())
            ffn_h(g, st, r_hbuf, hbuf, r_g1, g1buf, q1_s, it_a)
            ffn_h(g, st, r_g1, g1buf, r_hbuf, hbuf, q2_s, it_b)
            ffn_w3(g, st, it_c)
            assert st.next_c == NT
            if g == 0:
                merge_g0_res()

        SE.wait_cnt("aq")
        SQ.wait_cnt("sq")
        PO.wait_cnt("gq")

        engines = {"pe": PE, "ve": VE, "se": SE, "po": PO, "sq": SQ}
        with nc.Block() as block:
            def runner(ename):
                eng = engines[ename]
                def _run(e):
                    for item in eng.ops:
                        if item[0] == "w":
                            e.wait_ge(sems[item[1]], item[2])
                        else:
                            ins = item[1](e)
                            if item[2] is not None:
                                step = 16 if item[2] in QKEYS else 1
                                ins.then_inc(sems[item[2]], step)
                return _run
            block.tensor(runner("pe"))
            block.vector(runner("ve"))
            block.scalar(runner("se"))
            block.gpsimd(runner("po"))
            block.sync(runner("sq"))

    return nc


def _layout_inputs(inputs, seq_len):
    f = np.float32
    w1 = inputs["L0_ffn_w1_w"]; w2 = inputs["L0_ffn_w2_w"]
    h1 = inputs["L0_ffn_h1_w"]; h2 = inputs["L0_ffn_h2_w"]
    w3 = inputs["L0_ffn_w3_w"]

    def tiles(w, nm, nk):
        return np.ascontiguousarray(
            np.asarray(w, f).reshape(nm, 128, nk, 128).transpose(0, 3, 2, 1))
    wflat = np.concatenate([
        tiles(w1, MH, NJ).reshape(-1), tiles(w2, MH, NJ).reshape(-1),
        tiles(h1, MH, MH).reshape(-1), tiles(h2, MH, MH).reshape(-1),
        tiles(w3, NJ, MH).reshape(-1)])
    assert wflat.size == WTOT

    xpw = np.asarray(inputs["L0_attn_x_proj_w"], f)
    xpw_all = np.zeros((NJ, 128, 128), f)
    for k in range(NJ):
        blk = xpw[:, k * 128:(k + 1) * 128]
        xpw_all[k, :, 0:NS] = blk[DTR:DTR + NS, :].T
        xpw_all[k, :, 32:32 + NS] = blk[DTR + NS:DTR + 2 * NS, :].T
        xpw_all[k, :, 64:128] = blk[0:DTR, :].T
    xpw_host = np.ascontiguousarray(xpw_all.transpose(1, 0, 2).reshape(128, NJ * 128))

    dtw = np.asarray(inputs["L0_attn_dt_proj_w"], f)
    dtw_host = np.ascontiguousarray(dtw.T.reshape(DTR, NJ * 128))

    alog = np.ascontiguousarray(
        np.broadcast_to(np.asarray(inputs["L0_attn_A_log"], f), (128, NS)))
    dtb = np.ascontiguousarray(np.asarray(inputs["L0_attn_dt_proj_b"], f).reshape(NJ, 128).T)
    dvec = np.ascontiguousarray(np.asarray(inputs["L0_attn_D"], f).reshape(NJ, 128).T)
    s1 = np.ascontiguousarray(np.asarray(inputs["L0_ffn_w1_s"], f).reshape(MH, 128).T)
    s2 = np.ascontiguousarray(np.asarray(inputs["L0_ffn_w2_s"], f).reshape(MH, 128).T)
    q1 = np.ascontiguousarray(np.asarray(inputs["L0_ffn_h1_s"], f).reshape(MH, 128).T)
    q2 = np.ascontiguousarray(np.asarray(inputs["L0_ffn_h2_s"], f).reshape(MH, 128).T)
    s3 = np.ascontiguousarray(np.asarray(inputs["L0_ffn_w3_s"], f).reshape(NJ, 128).T)
    eye = np.eye(128, dtype=f)
    ones = np.ones((128, 1), f)

    return {
        "wflat": wflat, "xpw": xpw_host, "dtw": dtw_host, "alog": alog,
        "dtb": dtb, "dvec": dvec, "s1v": s1, "s2v": s2, "q1v": q1, "q2v": q2,
        "s3v": s3, "eye": eye, "onescol": ones,
    }


def kernel(**inputs):
    u = np.asarray(inputs["L0_attn_u"], np.float32)
    batch, seq_len, d = u.shape
    assert d == D
    shared = _layout_inputs(inputs, seq_len)
    nc = build_program(seq_len)
    in_maps = []
    for c in range(batch):
        m = dict(shared)
        m["x"] = np.ascontiguousarray(u[c])
        in_maps.append(m)
    res = run_bass_kernel_spmd(nc, in_maps, core_ids=list(range(batch)))
    out = np.stack([np.asarray(r["y"], np.float32) for r in res.results], axis=0)
    return out


# version 12
# speedup vs baseline: 1.1650x; 1.1650x over previous
import sys
sys.path.insert(0, "/opt/trn_rl_repo")
from contextlib import ExitStack
import numpy as np
import concourse.bass as bass
import concourse.mybir as mybir
from concourse.bass_utils import run_bass_kernel_spmd

F32 = mybir.dt.float32
BF16 = mybir.dt.bfloat16
AF = mybir.ActivationFunctionType
ALU = mybir.AluOpType

D = 1024
NJ = 8
NS = 16
DTR = 64
T = 256
TG = 512
DH = 4096
MH = DH // 128
EPS = 1e-5

WOFF = {}
_off = 0
for _name, _n in (("w1", D * DH), ("w2", D * DH), ("h1", DH * DH),
                  ("h2", DH * DH), ("w3", DH * D)):
    WOFF[_name] = _off
    _off += _n
WTOT = _off

TILE = 128 * 1024
BLK = {"w1": 128 * 1024, "w2": 128 * 1024, "h1": 128 * 4096,
       "h2": 128 * 4096, "w3": 128 * 4096}
TILE_TAB = []
for _m in range(MH):
    TILE_TAB.append(("w1", _m, 0))
    TILE_TAB.append(("w2", _m, 0))
for _m in range(MH):
    for _q in range(4):
        TILE_TAB.append(("h1", _m, _q))
for _m in range(MH):
    for _q in range(4):
        TILE_TAB.append(("h2", _m, _q))
for _m in range(NJ):
    for _q in range(4):
        TILE_TAB.append(("w3", _m, _q))
NT = len(TILE_TAB)
assert NT * TILE == WTOT

QKEYS = ("sq", "aq", "gq")
ALLKEYS = ("pe", "ve", "se", "po", "sq", "aq", "gq")


class Res:
    __slots__ = ("w", "r")
    def __init__(self):
        self.w = None
        self.r = []


class Eng:
    SELF_SYNC = ("ve", "se", "po")

    def __init__(self, key, cnt, default_q=None):
        self.key = key
        self.cnt = cnt
        self.ops = []
        self.hi = {}
        self.default_q = default_q
        self.self_sync = key in Eng.SELF_SYNC
        self.pending = []

    def _wait(self, ev):
        k, v = ev
        if k == self.key and not self.self_sync:
            return
        if self.hi.get(k, 0) >= v:
            return
        self.hi[k] = v
        self.ops.append(("w", k, v))

    def acquire(self, rd=(), wr=()):
        for b in rd:
            if b.w is not None:
                self._wait(b.w)
        for b in wr:
            if b.w is not None:
                self._wait(b.w)
            for ev in b.r:
                self._wait(ev)

    def op(self, fn, rd=(), wr=(), inc=True, q=None):
        self.acquire(rd=rd, wr=wr)
        q = q if q is not None else self.default_q
        key = q if q is not None else self.key
        step = 16 if q is not None else 1
        self.ops.append(("o", fn, key if inc else None))
        if not inc:
            assert not wr, "non-inc ops must not declare writes"
            self.pending.extend(rd)
            return None
        self.cnt[key] += step
        ev = (key, self.cnt[key])
        for b in rd:
            b.r.append(ev)
        for b in self.pending:
            b.r.append(ev)
        self.pending = []
        for b in wr:
            b.w = ev
            b.r = []
        return ev

    def wait_cnt(self, key):
        self._wait((key, self.cnt[key]))


def build_program(seq_len):
    NCH = seq_len // T
    NG = NCH // 2
    assert NCH % 2 == 0
    nc = bass.Bass("TRN2", target_bir_lowering=False,
                   detect_race_conditions=False)

    x_h = nc.dram_tensor("x", [seq_len, D], F32, kind="ExternalInput").ap()
    y_h = nc.dram_tensor("y", [seq_len, D], F32, kind="ExternalOutput").ap()
    wflat = nc.dram_tensor("wflat", [WTOT], F32, kind="ExternalInput").ap()
    xpw_h = nc.dram_tensor("xpw", [128, NJ * 128], F32, kind="ExternalInput").ap()
    dtw_h = nc.dram_tensor("dtw", [DTR, NJ * 128], F32, kind="ExternalInput").ap()
    alog_h = nc.dram_tensor("alog", [128, NS], F32, kind="ExternalInput").ap()
    dtb_h = nc.dram_tensor("dtb", [128, NJ], F32, kind="ExternalInput").ap()
    dvec_h = nc.dram_tensor("dvec", [128, NJ], F32, kind="ExternalInput").ap()
    s1_h = nc.dram_tensor("s1v", [128, MH], F32, kind="ExternalInput").ap()
    s2_h = nc.dram_tensor("s2v", [128, MH], F32, kind="ExternalInput").ap()
    q1_h = nc.dram_tensor("q1v", [128, MH], F32, kind="ExternalInput").ap()
    q2_h = nc.dram_tensor("q2v", [128, MH], F32, kind="ExternalInput").ap()
    s3_h = nc.dram_tensor("s3v", [128, NJ], F32, kind="ExternalInput").ap()
    eye_h = nc.dram_tensor("eye", [128, 128], F32, kind="ExternalInput").ap()
    ones_h = nc.dram_tensor("onescol", [128, 1], F32, kind="ExternalInput").ap()
    wbf = nc.dram_tensor("wbf", [WTOT], BF16, kind="Internal").ap()
    rowd = nc.dram_tensor("rowd", [4, T], F32, kind="Internal").ap()
    bcd = nc.dram_tensor("bcd", [4, NS * T], BF16, kind="Internal").ap()

    ctx = ExitStack()
    def sbuf(name, shape, dt=F32):
        return ctx.enter_context(nc.sbuf_tensor(name, shape, dt))
    def psum(name, shape, dt=F32):
        return ctx.enter_context(nc.psum_tensor(name, shape, dt))

    with ctx:
        x_tm = [sbuf(f"x_tm{i}", [128, D]) for i in range(2)]
        u_tm = sbuf("u_tm", [128, D])
        x2 = [sbuf(f"x2_{i}", [128, NJ * T]) for i in range(4)]
        u_ln = [sbuf(f"uln{i}", [128, NJ * T]) for i in range(2)]
        ulnb = [sbuf(f"ulnb{i}", [128, T], BF16) for i in range(2)]
        delta = [sbuf(f"delta{i}", [128, NJ * T]) for i in range(2)]
        wtile = [sbuf(f"wt{i}", [128, T]) for i in range(2)]
        dBu = [sbuf(f"dBu{i}", [128, T]) for i in range(2)]
        htile = [sbuf(f"ht{i}", [128, T]) for i in range(2)]
        b_all = sbuf("ball", [128, NS * T], BF16)
        c_all = sbuf("call", [128, NS * T], BF16)
        misc = [sbuf(f"misc{i}", [128, T]) for i in range(4)]
        zbuf = [sbuf(f"zb{i}", [128, T]) for i in range(2)]
        state = sbuf("state", [128, 128])
        sb6 = sbuf("sb6", [128, 20])
        mv = sbuf("mv", [128, 2])
        rstd1 = sbuf("rstd1", [128, 1])
        mr1 = sbuf("mr1", [128, 1])
        bcmp = [sbuf(f"bcmp{i}", [NS, T], BF16) for i in range(2)]
        ccmp = [sbuf(f"ccmp{i}", [NS, T], BF16) for i in range(2)]
        dtraw = [sbuf(f"dtraw{i}", [DTR, T], BF16) for i in range(2)]
        rrep = [sbuf(f"rrep{i}", [128, T]) for i in range(2)]
        ln2b = sbuf("ln2b", [128, NJ * TG], BF16)
        hbuf = sbuf("hbuf", [128, MH * TG], BF16)
        g1buf = sbuf("g1buf", [128, MH * TG], BF16)
        sa = [sbuf(f"sa{i}", [128, TG]) for i in range(1)]
        wslab = sbuf("wslab", [128, 8 * 1024], BF16)
        outev = [sbuf(f"outev{i}", [128, 128]) for i in range(2)]
        xpw_b = sbuf("xpwb", [128, NJ * 128], BF16)
        dtw_b = sbuf("dtwb", [DTR, NJ * 128], BF16)
        arep = sbuf("arep", [128, NS])
        dtb_s = sbuf("dtbs", [128, NJ])
        dvec_s = sbuf("dvecs", [128, NJ])
        s1_s = sbuf("s1s", [128, MH])
        s2_s = sbuf("s2s", [128, MH])
        q1_s = sbuf("q1s", [128, MH])
        q2_s = sbuf("q2s", [128, MH])
        s3_s = sbuf("s3s", [128, NJ])
        eye_s = sbuf("eyes", [128, 128])
        ones_s = sbuf("oness", [128, 1])
        eps_v = sbuf("epsv", [128, 1])

        wsl = [wslab[:, i * 1024:(i + 1) * 1024] for i in range(8)]
        wslab_f = wslab[:].bitcast(F32)
        wsl_f32 = [wslab_f[:, p * 1024:(p + 1) * 1024] for p in range(2)]

        pf = [psum(f"pf{i}", [128, TG]) for i in range(4)]
        pebt = psum("pebt", [128, TG])
        pdt = psum("pdt", [128, TG])
        pp = [psum(f"pp{i}", [128, TG]) for i in range(2)]
        pT = [pebt[:, 0:128], pebt[:, 128:256]]
        pX = pebt[:, 256:512]
        pD = [pdt[:, 0:256], pdt[:, 256:512]]

        sems = {k: ctx.enter_context(nc.semaphore(k)) for k in ALLKEYS}
        CNT = {k: 0 for k in ALLKEYS}
        PE = Eng("pe", CNT)
        VE = Eng("ve", CNT)
        SE = Eng("se", CNT)
        PO = Eng("po", CNT)
        SQ = Eng("sq", CNT, default_q="sq")

        r_xtm = [Res() for _ in range(2)]
        r_utm = Res()
        r_zb = [Res() for _ in range(2)]
        r_x2 = [Res() for _ in range(4)]
        r_uln = [Res() for _ in range(2)]
        r_ulnb = [Res() for _ in range(2)]
        r_delta = [Res() for _ in range(2)]
        r_wtile = [Res() for _ in range(2)]
        r_dBu = [Res() for _ in range(2)]
        r_h = [Res() for _ in range(2)]
        r_ball = Res()
        r_call = Res()
        r_misc = [Res() for _ in range(4)]
        r_state = Res()
        r_stat = Res()
        r_bcmp = [Res() for _ in range(2)]
        r_ccmp = [Res() for _ in range(2)]
        r_dtraw = [Res() for _ in range(2)]
        r_rowd = Res()
        r_bcd = [Res() for _ in range(2)]
        r_rrep = [Res() for _ in range(2)]
        r_ln2b = Res()
        r_hbuf = [Res() for _ in range(MH)]
        r_g1 = [Res() for _ in range(MH)]
        r_sa = [Res() for _ in range(1)]
        r_wsl = [Res() for _ in range(8)]
        r_fp = [Res() for _ in range(2)]
        r_outev = [Res() for _ in range(2)]
        r_pf = [Res() for _ in range(4)]
        r_pT = [Res() for _ in range(2)]
        r_pX = Res()
        r_pD = [Res() for _ in range(2)]
        r_pp = [Res() for _ in range(2)]
        r_consts = Res()

        for dst, src in ((arep[:], alog_h), (dtb_s[:], dtb_h), (dvec_s[:], dvec_h),
                         (s1_s[:], s1_h), (s2_s[:], s2_h), (q1_s[:], q1_h),
                         (q2_s[:], q2_h), (s3_s[:], s3_h),
                         (eye_s[:], eye_h), (ones_s[:], ones_h)):
            SQ.op(lambda e, d=dst, s=src: e.dma_start(out=d, in_=s), wr=[r_consts])
        SQ.op(lambda e: e.dma_start(out=x2[2][:, 0:1024], in_=xpw_h), wr=[r_x2[2]])
        SE.op(lambda e: e.copy(xpw_b[:], x2[2][:, 0:1024]),
              rd=[r_x2[2]], wr=[r_consts])
        SQ.op(lambda e: e.dma_start(out=x2[3][0:DTR, 0:1024], in_=dtw_h),
              wr=[r_x2[3]])
        SE.op(lambda e: e.copy(dtw_b[:], x2[3][0:DTR, 0:1024]),
              rd=[r_x2[3]], wr=[r_consts])
        SE.op(lambda e: e.activation(arep[:], arep[:], AF.Exp),
              rd=[r_consts], wr=[r_consts])
        VE.op(lambda e: e.tensor_scalar_mul(arep[:], arep[:], -1.0),
              rd=[r_consts], wr=[r_consts])
        VE.op(lambda e: e.memset(state[:], 0.0), wr=[r_state])
        VE.op(lambda e: e.memset(eps_v[:], EPS), wr=[r_consts])

        def tile_src_f32(i):
            name, m, q = TILE_TAB[i]
            base = WOFF[name] + m * BLK[name]
            ap = wflat[base: base + BLK[name]].rearrange("(p f) -> p f", p=128)
            return ap[:, q * 1024:(q + 1) * 1024]

        def tile_wbf(i):
            return wbf[i * TILE:(i + 1) * TILE].rearrange("(p f) -> p f", p=128)

        store_ev = [None] * NT

        class WStream:
            def __init__(self, g):
                self.g = g
                self.next_p = 0
                self.next_c = 0

            def produce_one(self):
                i = self.next_p
                self.next_p += 1
                if i >= NT:
                    return
                if self.g == 0:
                    p = i % 2
                    oslot = 4 + i % 4
                    src = tile_src_f32(i)
                    SQ.op(lambda e, src=src, p=p: e.dma_start(
                        out=wsl_f32[p], in_=src), wr=[r_fp[p]])
                    PO.op(lambda e, p=p, oslot=oslot: e.tensor_copy(
                        out=wsl[oslot], in_=wsl_f32[p]),
                        rd=[r_fp[p]], wr=[r_wsl[oslot]])
                    store_ev[i] = SE.op(lambda e, i=i, oslot=oslot: e.dma_start(
                        out=tile_wbf(i), in_=wsl[oslot]),
                        rd=[r_wsl[oslot]], q="aq")
                else:
                    slot = i % 8
                    if self.g == 1:
                        SQ._wait(store_ev[i])
                    SQ.op(lambda e, i=i, slot=slot: e.dma_start(
                        out=wsl[slot], in_=tile_wbf(i)), wr=[r_wsl[slot]])

            def consume(self):
                i = self.next_c
                self.next_c += 1
                la = 4 if self.g == 0 else 7
                while self.next_p < min(i + la, NT):
                    self.produce_one()
                slot = (4 + i % 4) if self.g == 0 else i % 8
                return wsl[slot], r_wsl[slot]

        def merge_g0_res():
            for p in range(2):
                for sl in (2 * p, 2 * p + 1):
                    r_wsl[sl].w = r_fp[p].w
                    r_wsl[sl].r = list(r_fp[p].r)

        def prep_chunk(k):
            p = k % 2
            x2s, rx2 = x2[k % 4], r_x2[k % 4]
            for c in range(2):
                src_ = x_h[k * T + c * 128: k * T + (c + 1) * 128, :]
                PO.op(lambda e, c=c, s=src_: e.dma_start(out=x_tm[c][:], in_=s),
                      rd=[], wr=[r_xtm[c]], q="gq")
                xt = x_tm[c]
                VE.op(lambda e, xt=xt: e.bn_stats(sb6[:, 0:6], xt[:, 0:512]),
                      rd=[r_xtm[c]], wr=[r_stat])
                VE.op(lambda e, xt=xt: e.bn_stats(sb6[:, 6:12], xt[:, 512:1024]),
                      rd=[r_xtm[c]], wr=[r_stat])
                s6v = sb6[:, 0:12].rearrange("p (g k) -> p g k", k=3)
                means_ap = s6v[:, :, 1]
                cv_ap = s6v[:, :, 2]
                VE.op(lambda e, means_ap=means_ap: e.reduce_sum(
                    mv[:, 0:1], means_ap, mybir.AxisListType.X),
                    rd=[r_stat], wr=[r_stat])
                VE.op(lambda e, means_ap=means_ap: e.tensor_mul(
                    sb6[:, 12:16], means_ap, means_ap), rd=[r_stat], wr=[r_stat])
                VE.op(lambda e: e.reduce_sum(
                    sb6[:, 17:18], sb6[:, 12:16], mybir.AxisListType.X),
                    rd=[r_stat], wr=[r_stat])
                VE.op(lambda e, cv_ap=cv_ap: e.reduce_sum(
                    sb6[:, 16:17], cv_ap, mybir.AxisListType.X),
                    rd=[r_stat], wr=[r_stat])
                VE.op(lambda e: e.tensor_scalar_mul(mv[:, 0:1], mv[:, 0:1], 0.25),
                      rd=[r_stat], wr=[r_stat])
                VE.op(lambda e: e.tensor_scalar_mul(sb6[:, 18:19], sb6[:, 16:17],
                                                    1.0 / D),
                      rd=[r_stat], wr=[r_stat])
                VE.op(lambda e: e.scalar_tensor_tensor(
                    mv[:, 1:2], sb6[:, 17:18], 0.25, sb6[:, 18:19],
                    ALU.mult, ALU.add), rd=[r_stat], wr=[r_stat])
                VE.op(lambda e: e.tensor_mul(sb6[:, 19:20], mv[:, 0:1], mv[:, 0:1]),
                      rd=[r_stat], wr=[r_stat])
                VE.op(lambda e: e.tensor_sub(mv[:, 1:2], mv[:, 1:2], sb6[:, 19:20]),
                      rd=[r_stat], wr=[r_stat])
                for j in range(NJ):
                    bank = j % 2
                    PE.acquire(rd=[r_xtm[c], r_consts], wr=[r_pp[bank]])
                    PE.op(lambda e, c=c, j=j, bank=bank: e.transpose(
                        pp[bank][:, 0:128], x_tm[c][:, j * 128:(j + 1) * 128],
                        eye_s[:]),
                        rd=[r_xtm[c]], wr=[r_pp[bank]])
                    SE.op(lambda e, j=j, c=c, bank=bank, x2s=x2s: e.copy(
                        x2s[:, j * T + c * 128: j * T + (c + 1) * 128],
                        pp[bank][:, 0:128]),
                        rd=[r_pp[bank]], wr=[rx2])
                SE.op(lambda e: e.activation(rstd1[:], mv[:, 1:2], AF.Ln, bias=eps_v[:, 0:1]),
                      rd=[r_stat], wr=[r_stat])
                SE.op(lambda e: e.activation(rstd1[:], rstd1[:], AF.Exp, scale=-0.5),
                      rd=[r_stat], wr=[r_stat])
                VE.op(lambda e: e.scalar_tensor_tensor(mr1[:], mv[:, 0:1], -1.0,
                                                       rstd1[:], ALU.mult, ALU.mult),
                      rd=[r_stat], wr=[r_stat])
                VE.op(lambda e, xt=xt: e.tensor_scalar(
                    u_tm[:], xt[:], rstd1[:, 0:1], mr1[:, 0:1], ALU.mult, ALU.add),
                    rd=[r_stat, r_xtm[c]], wr=[r_utm])
                for j in range(NJ):
                    bank = j % 2
                    PE.acquire(rd=[r_utm, r_consts], wr=[r_pp[bank]])
                    PE.op(lambda e, j=j, bank=bank: e.transpose(
                        pp[bank][:, 128:256], u_tm[:, j * 128:(j + 1) * 128],
                        eye_s[:]),
                        rd=[r_utm], wr=[r_pp[bank]])
                    SE.op(lambda e, j=j, c=c, p=p, bank=bank: e.copy(
                        u_ln[p][:, j * T + c * 128: j * T + (c + 1) * 128],
                        pp[bank][:, 128:256]),
                        rd=[r_pp[bank]], wr=[r_uln[p]])
            for j in range(NJ):
                SE.op(lambda e, j=j, p=p: e.copy(
                    ulnb[j % 2][:], u_ln[p][:, j * T:(j + 1) * T]),
                    rd=[r_uln[p]], wr=[r_ulnb[j % 2]])
                if j == 0:
                    PE.acquire(wr=[r_pX])
                PE.op(lambda e, j=j: e.matmul(
                    pX, xpw_b[:, j * 128:(j + 1) * 128],
                    ulnb[j % 2][:], start=(j == 0), stop=(j == NJ - 1)),
                    rd=[r_ulnb[j % 2], r_consts],
                    wr=[r_pX] if j == NJ - 1 else (), inc=True)
            SE.op(lambda e, p=p: e.copy(bcmp[p][:], pebt[0:NS, 256:512]),
                  rd=[r_pX], wr=[r_bcmp[p]])
            SE.op(lambda e, p=p: e.copy(ccmp[p][:], pebt[32:48, 256:512]),
                  rd=[r_pX], wr=[r_ccmp[p]])
            SE.op(lambda e, p=p: e.copy(dtraw[p][:], pebt[64:128, 256:512]),
                  rd=[r_pX], wr=[r_dtraw[p]])
            SE.op(lambda e, p=p: e.dma_start(
                out=bcd[2 * p:2 * p + 1, :].rearrange("o (n t) -> (o n) t", t=T),
                in_=bcmp[p][:]),
                rd=[r_bcmp[p]], wr=[r_bcd[p]], q="aq")
            SE.op(lambda e, p=p: e.dma_start(
                out=bcd[2 * p + 1:2 * p + 2, :].rearrange("o (n t) -> (o n) t", t=T),
                in_=ccmp[p][:]),
                rd=[r_ccmp[p]], wr=[r_bcd[p]], q="aq")
            for j in range(NJ):
                dslot = j % 2
                ma, rma = misc[2 * (j % 2)], r_misc[2 * (j % 2)]
                mb, rmb = misc[2 * (j % 2) + 1], r_misc[2 * (j % 2) + 1]
                PE.op(lambda e, j=j, p=p, dslot=dslot: e.matmul(
                    pD[dslot], dtw_b[:, j * 128:(j + 1) * 128], dtraw[p][:],
                    start=True, stop=True),
                    rd=[r_dtraw[p], r_consts], wr=[r_pD[dslot]])
                VE.op(lambda e, j=j, dslot=dslot, ma=ma: e.tensor_scalar(
                    ma[:], pD[dslot], dtb_s[:, j:j + 1], None,
                    ALU.add),
                    rd=[r_pD[dslot], r_consts], wr=[rma])
                SE.op(lambda e, ma=ma, mb=mb: e.activation(mb[:], ma[:], AF.Relu),
                      rd=[rma], wr=[rmb])
                SE.op(lambda e, ma=ma: e.activation(ma[:], ma[:], AF.Abs),
                      rd=[rma], wr=[rma])
                SE.op(lambda e, ma=ma: e.activation(ma[:], ma[:], AF.Exp, scale=-1.0),
                      rd=[rma], wr=[rma])
                SE.op(lambda e, ma=ma: e.activation(ma[:], ma[:], AF.Ln, bias=1.0),
                      rd=[rma], wr=[rma])
                VE.op(lambda e, j=j, p=p, ma=ma, mb=mb: e.tensor_add(
                    delta[p][:, j * T:(j + 1) * T], mb[:], ma[:]),
                    rd=[rmb, rma], wr=[r_delta[p]])

        def visit_steps(k, j):
            p = k % 2
            js = j % 2
            x2s, rx2 = x2[k % 4], r_x2[k % 4]
            jT = slice(j * T, (j + 1) * T)
            ADD_ENG = VE

            def head():
                VE.op(lambda e: e.tensor_mul(
                    wtile[js][:], delta[p][:, jT], u_ln[p][:, jT]),
                    rd=[r_delta[p], r_uln[p]], wr=[r_wtile[js]])
                VE.op(lambda e: e.scalar_tensor_tensor(
                    x2s[:, jT], u_ln[p][:, jT], dvec_s[:, j:j + 1], x2s[:, jT],
                    ALU.mult, ALU.add),
                    rd=[r_uln[p], r_consts, rx2], wr=[rx2])

            def emit_exp(n):
                bank = (n // 2) % 2
                dcol = slice((n % 2) * T, (n % 2 + 1) * T)
                SE.op(lambda e, dcol=dcol, n=n, bank=bank: e.activation(
                    pp[bank][:, dcol], delta[p][:, jT], AF.Exp,
                    scale=arep[:, n:n + 1]),
                    rd=[r_delta[p], r_consts], wr=[r_pp[bank]])

            def emit_scan(n):
                bank = (n // 2) % 2
                dcol = slice((n % 2) * T, (n % 2 + 1) * T)
                col = j * NS + n
                db, rdb = dBu[n % 2], r_dBu[n % 2]
                hs, rh = htile[n % 2], r_h[n % 2]
                PO.op(lambda e, nT=slice(n * T, (n + 1) * T), db=db:
                      e.tensor_mul(db[:], wtile[js][:], b_all[:, nT]),
                      rd=[r_wtile[js], r_ball], wr=[rdb])
                VE.op(lambda e, dcol=dcol, bank=bank, db=db, hs=hs,
                      col=col: e.tensor_tensor_scan(
                    hs[:], pp[bank][:, dcol], db[:], state[:, col:col + 1],
                    ALU.mult, ALU.add),
                    rd=[r_pp[bank], rdb, r_state], wr=[rh])

            def emit_z(n):
                zs = n % 2
                hs, rh = htile[n % 2], r_h[n % 2]
                PO.op(lambda e, nT=slice(n * T, (n + 1) * T), hs=hs, zs=zs:
                      e.tensor_mul(zbuf[zs][:], hs[:], c_all[:, nT]),
                      rd=[rh, r_call], wr=[r_zb[zs]])
                PO.op(lambda e, hs=hs, col=j * NS + n: e.tensor_copy(
                    out=state[:, col:col + 1], in_=hs[:, T - 1:T]),
                    rd=[rh], wr=[r_state])

            def emit_add(n):
                zs = n % 2
                ADD_ENG.op(lambda e, zs=zs: e.tensor_add(
                    x2s[:, jT], x2s[:, jT], zbuf[zs][:]),
                    rd=[r_zb[zs], rx2], wr=[rx2])

            steps = [head, lambda: (emit_exp(0), emit_exp(1)),
                     lambda: (emit_exp(2), emit_exp(3))]
            for n in range(NS):
                def mid(n=n):
                    emit_scan(n)
                    if n >= 1:
                        emit_z(n - 1)
                    if n >= 2:
                        emit_add(n - 2)
                        if n + 2 < NS:
                            emit_exp(n + 2)
                steps.append(mid)
            def tail():
                emit_z(NS - 1)
                emit_add(NS - 2)
                emit_add(NS - 1)
            steps.append(tail)
            return steps

        def chunk_steps(k):
            p = k % 2
            steps = []
            def bc_bcast():
                PO.op(lambda e: e.dma_start(
                    out=b_all[:],
                    in_=bcd[2 * p:2 * p + 1, :].broadcast_to([128, NS * T])),
                    rd=[r_bcd[p]], wr=[r_ball], q="gq")
                PO.op(lambda e: e.dma_start(
                    out=c_all[:],
                    in_=bcd[2 * p + 1:2 * p + 2, :].broadcast_to([128, NS * T])),
                    rd=[r_bcd[p]], wr=[r_call], q="gq")
            steps.append(bc_bcast)
            for j in range(NJ):
                steps.extend(visit_steps(k, j))
            return steps

        def pace(it, n):
            done = 0
            for fn in it:
                fn()
                done += 1
                if done >= n:
                    break

        def ln2_chunk(k):
            p = k % 2
            x2s, rx2 = x2[k % 4], r_x2[k % 4]

            def s_sum():
                PE.acquire(wr=[r_pD[0], r_pD[1]])
                for j in range(NJ):
                    jT = slice(j * T, (j + 1) * T)
                    PE.op(lambda e, jT=jT, j=j: e.matmul(
                        pdt[0:1, 0:T], ones_s[:, 0:1], x2s[:, jT],
                        start=(j == 0), stop=(j == NJ - 1)),
                        rd=[rx2, r_consts], wr=(), inc=(j == NJ - 1))

            def s_sq():
                for j in range(NJ):
                    jT = slice(j * T, (j + 1) * T)
                    sq, rsq = misc[j % 4], r_misc[j % 4]
                    SE.op(lambda e, jT=jT, sq=sq: e.activation(
                        sq[:], x2s[:, jT], AF.Square), rd=[rx2], wr=[rsq])
                    PE.op(lambda e, j=j, sq=sq: e.matmul(
                        pdt[0:1, 256:256 + T], ones_s[:, 0:1], sq[:],
                        start=(j == 0), stop=(j == NJ - 1)),
                        rd=[rsq, r_consts],
                        wr=[r_pD[0], r_pD[1]] if j == NJ - 1 else (),
                        inc=(j == NJ - 1))

            def s_rows():
                r0 = misc[2][0:1, 0:T]
                r1_ = misc[3][0:1, 0:T]
                r2 = rrep[0][0:1, 0:T]
                SE.op(lambda e, r0=r0: e.copy(r0, pdt[0:1, 0:T]),
                      rd=[r_pD[0]], wr=[r_misc[2]])
                SE.op(lambda e, r1_=r1_: e.copy(r1_, pdt[0:1, 256:256 + T]),
                      rd=[r_pD[1]], wr=[r_misc[3]])
                VE.op(lambda e: e.tensor_scalar_mul(r0, r0, 1.0 / D),
                      rd=[r_misc[2]], wr=[r_misc[2]])
                VE.op(lambda e: e.tensor_mul(r2, r0, r0),
                      rd=[r_misc[2]], wr=[r_rrep[0]])
                VE.op(lambda e: e.scalar_tensor_tensor(r1_, r1_, 1.0 / D, r2,
                                                       ALU.mult, ALU.subtract),
                      rd=[r_misc[3], r_rrep[0]], wr=[r_misc[3]])
                SE.op(lambda e: e.activation(r1_, r1_, AF.Ln, bias=eps_v[0:1, 0:1]),
                      rd=[r_misc[3]], wr=[r_misc[3]])
                SE.op(lambda e: e.activation(r1_, r1_, AF.Exp, scale=-0.5),
                      rd=[r_misc[3]], wr=[r_misc[3]])
                VE.op(lambda e: e.scalar_tensor_tensor(r2, r0, -1.0, r1_,
                                                       ALU.mult, ALU.mult),
                      rd=[r_misc[2], r_misc[3]], wr=[r_rrep[0]])
                SE.op(lambda e: e.dma_start(out=rowd[2 * p:2 * p + 1, :], in_=r1_),
                      rd=[r_misc[3]], wr=[r_rowd], q="aq")
                SE.op(lambda e: e.dma_start(out=rowd[2 * p + 1:2 * p + 2, :], in_=r2),
                      rd=[r_rrep[0]], wr=[r_rowd], q="aq")
                SE.op(lambda e: e.dma_start(
                    out=rrep[0][:],
                    in_=rowd[2 * p:2 * p + 1, :].broadcast_to([128, T])),
                    rd=[r_rowd], wr=[r_rrep[0]], q="aq")
                SE.op(lambda e: e.dma_start(
                    out=rrep[1][:],
                    in_=rowd[2 * p + 1:2 * p + 2, :].broadcast_to([128, T])),
                    rd=[r_rowd], wr=[r_rrep[1]], q="aq")

            def s_scale(j0, j1):
                for j in range(j0, j1):
                    jT = slice(j * T, (j + 1) * T)
                    tmp, rtmp = misc[j % 4], r_misc[j % 4]
                    VE.op(lambda e, jT=jT, tmp=tmp: e.tensor_mul(
                        tmp[:], x2s[:, jT], rrep[0][:]),
                        rd=[rx2, r_rrep[0]], wr=[rtmp])
                    dst = ln2b[:, j * TG + p * T: j * TG + (p + 1) * T]
                    VE.op(lambda e, tmp=tmp, dst=dst: e.tensor_add(
                        dst, tmp[:], rrep[1][:]),
                        rd=[rtmp, r_rrep[1]], wr=[r_ln2b])

            return [s_sum, s_sq, s_rows,
                    lambda: s_scale(0, 4), lambda: s_scale(4, NJ)]

        def ffn_w1w2(g, st):
            for m in range(MH):
                wv1, rw1 = st.consume()
                pb1, pb2 = m % 2, 2 + m % 2
                PE.acquire(wr=[r_pf[pb1]])
                for k in range(NJ):
                    last = k == NJ - 1
                    PE.op(lambda e, k=k, pb1=pb1, wv1=wv1, last=last: e.matmul(
                        pf[pb1][:], wv1[:, k * 128:(k + 1) * 128],
                        ln2b[:, k * TG:(k + 1) * TG], start=(k == 0), stop=last),
                        rd=[rw1, r_ln2b], wr=[r_pf[pb1]] if last else (),
                        inc=last)
                wv2, rw2 = st.consume()
                PE.acquire(wr=[r_pf[pb2]])
                for k in range(NJ):
                    last = k == NJ - 1
                    PE.op(lambda e, k=k, pb2=pb2, wv2=wv2, last=last: e.matmul(
                        pf[pb2][:], wv2[:, k * 128:(k + 1) * 128],
                        ln2b[:, k * TG:(k + 1) * TG], start=(k == 0), stop=last),
                        rd=[rw2, r_ln2b], wr=[r_pf[pb2]] if last else (),
                        inc=last)
                SE.op(lambda e, m=m, pb1=pb1: e.activation(
                    sa[0][:], pf[pb1][:], AF.Silu, scale=s1_s[:, m:m + 1]),
                    rd=[r_pf[pb1], r_consts], wr=[r_sa[0]])
                VE.op(lambda e, m=m, pb2=pb2: e.scalar_tensor_tensor(
                    hbuf[:, m * TG:(m + 1) * TG], pf[pb2][:], s2_s[:, m:m + 1],
                    sa[0][:], ALU.mult, ALU.mult),
                    rd=[r_pf[pb2], r_consts, r_sa[0]], wr=[r_hbuf[m]])

        def ffn_h(g, st, src_res, src_buf, dst_res, dst_buf, qscale, steps_it):
            for m in range(MH):
                pb = m % 4
                PE.acquire(wr=[r_pf[pb]])
                for quarter in range(4):
                    wv, rw = st.consume()
                    for kk in range(MH // 4):
                        k = quarter * (MH // 4) + kk
                        last = k == MH - 1
                        PE.op(lambda e, kk=kk, k=k, pb=pb, wv=wv, last=last,
                              sb_=src_buf: e.matmul(
                            pf[pb][:], wv[:, kk * 128:(kk + 1) * 128],
                            sb_[:, k * TG:(k + 1) * TG], start=(k == 0), stop=last),
                            rd=[rw, src_res[k]],
                            wr=[r_pf[pb]] if last else (), inc=last)
                SE.op(lambda e, m=m, pb=pb, db=dst_buf, qs=qscale: e.activation(
                    db[:, m * TG:(m + 1) * TG], pf[pb][:], AF.Gelu,
                    scale=qs[:, m:m + 1]),
                    rd=[r_pf[pb], r_consts], wr=[dst_res[m]])
                pace(steps_it, 6)
            pace(steps_it, 10 ** 9)

        def ffn_w3(g, st, steps_it):
            def epilogue(m, pb):
                for c in range(2):
                    kc = 2 * g + c
                    x2s, rx2 = x2[kc % 4], r_x2[kc % 4]
                    VE.op(lambda e, m=m, c=c, pb=pb, x2s=x2s: e.scalar_tensor_tensor(
                        x2s[:, m * T:(m + 1) * T], pf[pb][:, c * T:(c + 1) * T],
                        s3_s[:, m:m + 1], x2s[:, m * T:(m + 1) * T],
                        ALU.mult, ALU.add),
                        rd=[r_pf[pb], r_consts, rx2], wr=[rx2])
                for cb in range(4):
                    c, ci = cb // 2, cb % 2
                    kc = 2 * g + c
                    x2s, rx2 = x2[kc % 4], r_x2[kc % 4]
                    bank = cb % 2
                    PE.acquire(rd=[rx2, r_consts], wr=[r_pp[bank]])
                    PE.op(lambda e, m=m, ci=ci, bank=bank, x2s=x2s: e.transpose(
                        pp[bank][:, 0:128],
                        x2s[:, m * T + ci * 128: m * T + (ci + 1) * 128], eye_s[:]),
                        rd=[rx2, r_consts], wr=[r_pp[bank]])
                    SE.op(lambda e, cb=cb, bank=bank: e.copy(
                        outev[cb % 2][:], pp[bank][:, 0:128]),
                        rd=[r_pp[bank]], wr=[r_outev[cb % 2]])
                    dst = y_h[g * TG + cb * 128: g * TG + (cb + 1) * 128,
                              m * 128:(m + 1) * 128]
                    SE.op(lambda e, d=dst, cb=cb: e.dma_start(
                        out=d, in_=outev[cb % 2][:]),
                        rd=[r_outev[cb % 2]], q="aq")

            pend = None
            for m in range(NJ):
                pb = m % 2
                PE.acquire(wr=[r_pf[pb]])
                for quarter in range(4):
                    wv, rw = st.consume()
                    for kk in range(MH // 4):
                        k = quarter * (MH // 4) + kk
                        last = k == MH - 1
                        PE.op(lambda e, kk=kk, k=k, pb=pb, wv=wv,
                              last=last: e.matmul(
                            pf[pb][:], wv[:, kk * 128:(kk + 1) * 128],
                            hbuf[:, k * TG:(k + 1) * TG], start=(k == 0), stop=last),
                            rd=[rw, r_hbuf[k]],
                            wr=[r_pf[pb]] if last else (), inc=last)
                if pend is not None:
                    pend()
                pend = (lambda m=m, pb=pb: epilogue(m, pb))
                pace(steps_it, 2)
            pend()
            pace(steps_it, 10 ** 9)

        prep_chunk(0)
        for fn in chunk_steps(0):
            fn()
        for fn in ln2_chunk(0):
            fn()
        prep_chunk(1)
        for fn in chunk_steps(1):
            fn()
        for fn in ln2_chunk(1):
            fn()

        for g in range(NG):
            st = WStream(g)
            a, b = 2 * g + 2, 2 * g + 3
            ffn_w1w2(g, st)
            if a < NCH:
                prep_chunk(a)
                prep_chunk(b)
                it_a = iter(chunk_steps(a))
                it_b = iter(ln2_chunk(a) + chunk_steps(b))
                it_c = iter(ln2_chunk(b))
            else:
                it_a = it_b = it_c = iter(())
            ffn_h(g, st, r_hbuf, hbuf, r_g1, g1buf, q1_s, it_a)
            ffn_h(g, st, r_g1, g1buf, r_hbuf, hbuf, q2_s, it_b)
            ffn_w3(g, st, it_c)
            assert st.next_c == NT
            if g == 0:
                merge_g0_res()

        SE.wait_cnt("aq")
        SQ.wait_cnt("sq")
        PO.wait_cnt("gq")

        engines = {"pe": PE, "ve": VE, "se": SE, "po": PO, "sq": SQ}
        with nc.Block() as block:
            def runner(ename):
                eng = engines[ename]
                def _run(e):
                    for item in eng.ops:
                        if item[0] == "w":
                            e.wait_ge(sems[item[1]], item[2])
                        else:
                            ins = item[1](e)
                            if item[2] is not None:
                                step = 16 if item[2] in QKEYS else 1
                                ins.then_inc(sems[item[2]], step)
                return _run
            block.tensor(runner("pe"))
            block.vector(runner("ve"))
            block.scalar(runner("se"))
            block.gpsimd(runner("po"))
            block.sync(runner("sq"))

    return nc


def _layout_inputs(inputs, seq_len):
    f = np.float32
    w1 = inputs["L0_ffn_w1_w"]; w2 = inputs["L0_ffn_w2_w"]
    h1 = inputs["L0_ffn_h1_w"]; h2 = inputs["L0_ffn_h2_w"]
    w3 = inputs["L0_ffn_w3_w"]

    def tiles(w, nm, nk):
        return np.ascontiguousarray(
            np.asarray(w, f).reshape(nm, 128, nk, 128).transpose(0, 3, 2, 1))
    wflat = np.concatenate([
        tiles(w1, MH, NJ).reshape(-1), tiles(w2, MH, NJ).reshape(-1),
        tiles(h1, MH, MH).reshape(-1), tiles(h2, MH, MH).reshape(-1),
        tiles(w3, NJ, MH).reshape(-1)])
    assert wflat.size == WTOT

    xpw = np.asarray(inputs["L0_attn_x_proj_w"], f)
    xpw_all = np.zeros((NJ, 128, 128), f)
    for k in range(NJ):
        blk = xpw[:, k * 128:(k + 1) * 128]
        xpw_all[k, :, 0:NS] = blk[DTR:DTR + NS, :].T
        xpw_all[k, :, 32:32 + NS] = blk[DTR + NS:DTR + 2 * NS, :].T
        xpw_all[k, :, 64:128] = blk[0:DTR, :].T
    xpw_host = np.ascontiguousarray(xpw_all.transpose(1, 0, 2).reshape(128, NJ * 128))

    dtw = np.asarray(inputs["L0_attn_dt_proj_w"], f)
    dtw_host = np.ascontiguousarray(dtw.T.reshape(DTR, NJ * 128))

    alog = np.ascontiguousarray(
        np.broadcast_to(np.asarray(inputs["L0_attn_A_log"], f), (128, NS)))
    dtb = np.ascontiguousarray(np.asarray(inputs["L0_attn_dt_proj_b"], f).reshape(NJ, 128).T)
    dvec = np.ascontiguousarray(np.asarray(inputs["L0_attn_D"], f).reshape(NJ, 128).T)
    s1 = np.ascontiguousarray(np.asarray(inputs["L0_ffn_w1_s"], f).reshape(MH, 128).T)
    s2 = np.ascontiguousarray(np.asarray(inputs["L0_ffn_w2_s"], f).reshape(MH, 128).T)
    q1 = np.ascontiguousarray(np.asarray(inputs["L0_ffn_h1_s"], f).reshape(MH, 128).T)
    q2 = np.ascontiguousarray(np.asarray(inputs["L0_ffn_h2_s"], f).reshape(MH, 128).T)
    s3 = np.ascontiguousarray(np.asarray(inputs["L0_ffn_w3_s"], f).reshape(NJ, 128).T)
    eye = np.eye(128, dtype=f)
    ones = np.ones((128, 1), f)

    return {
        "wflat": wflat, "xpw": xpw_host, "dtw": dtw_host, "alog": alog,
        "dtb": dtb, "dvec": dvec, "s1v": s1, "s2v": s2, "q1v": q1, "q2v": q2,
        "s3v": s3, "eye": eye, "onescol": ones,
    }


def kernel(**inputs):
    u = np.asarray(inputs["L0_attn_u"], np.float32)
    batch, seq_len, d = u.shape
    assert d == D
    shared = _layout_inputs(inputs, seq_len)
    nc = build_program(seq_len)
    in_maps = []
    for c in range(batch):
        m = dict(shared)
        m["x"] = np.ascontiguousarray(u[c])
        in_maps.append(m)
    res = run_bass_kernel_spmd(nc, in_maps, core_ids=list(range(batch)))
    out = np.stack([np.asarray(r["y"], np.float32) for r in res.results], axis=0)
    return out
